# version 12
# baseline (speedup 1.0000x reference)
import os
import numpy as np
import ml_dtypes
import concourse.bass as bass
import concourse.mybir as mybir
from concourse.bass_utils import run_bass_kernel_spmd
from contextlib import ExitStack

BF = ml_dtypes.bfloat16
F32 = mybir.dt.float32
BF16 = mybir.dt.bfloat16
AF = mybir.ActivationFunctionType
ALU = mybir.AluOpType
AX = mybir.AxisListType

D = 4096
TCORE = 2048
CH = 512
NCHUNK = int(os.environ.get("MK_NCHUNK", "4"))
HALO = 512
NQ, NKV, HD, GQ = 32, 8, 128, 4
ASCALE = float(1.0 / np.sqrt(np.float32(128.0)))
NEG = -30000.0
EPS = 1e-5
DBG = int(os.environ.get("MK_DBG", "0"))
DBGC = int(os.environ.get("MK_DBGC", "0"))
NOBIAS = int(os.environ.get("MK_NOBIAS", "0"))
XLOAD_SCALAR = int(os.environ.get("MK_XQ", "1"))

ENGS = ("sync", "gpsimd", "vector", "scalar", "tensor")


class Sched:
    def __init__(self):
        self.q = {e: [] for e in ENGS}
        self.sem = {}
        self.cnt = {}
        self.trace = []

    def add_sem(self, name, handle):
        self.sem[name] = handle
        self.cnt[name] = 0

    def wait(self, eng, tok):
        if tok is None:
            return
        if tok[0] == "LOADED":
            tok = tok[1]
        sname, val = tok
        if val <= 0:
            return
        h = self.sem[sname]
        self.trace.append((eng, "wait", sname, val, None))
        self.q[eng].append(lambda e, h=h, v=val: e.wait_ge(h, v))

    def wait_lazy(self, eng, sname, cell, key):
        h = self.sem[sname]
        self.trace.append((eng, "waitlazy", sname, cell, key))

        def f(e, h=h, cell=cell, key=key):
            v = cell.get(key, 0)
            if v > 0:
                e.wait_ge(h, v)

        self.q[eng].append(f)

    def inc(self, eng, sname, fn, n=1):
        h = self.sem[sname]
        self.cnt[sname] += n
        tok = self.cnt[sname]
        self.trace.append((eng, "inc", sname, n, None))
        self.q[eng].append(lambda e, fn=fn, h=h, n=n: fn(e).then_inc(h, n))
        return (sname, tok)

    def emit(self, eng, fn):
        self.trace.append((eng, "op", None, None, None))
        self.q[eng].append(fn)


LAST_SCHED = [None]


def build_nc():
    nc = bass.Bass("TRN2", target_bir_lowering=False)
    S = Sched()
    LAST_SCHED[0] = S

    xt = nc.dram_tensor("xt", [HALO + TCORE, D], F32, kind="ExternalInput").ap()
    wq_d = nc.dram_tensor("wq", [32, 128, 32, 128], F32, kind="ExternalInput").ap()
    wk_d = nc.dram_tensor("wk", [8, 128, 32, 128], F32, kind="ExternalInput").ap()
    wv_d = nc.dram_tensor("wv", [2, 128, 32, 512], F32, kind="ExternalInput").ap()
    wo_d = nc.dram_tensor("wo", [8, 128, 32, 512], F32, kind="ExternalInput").ap()
    Wt_d = nc.dram_tensor("Wt", [8, 128, 32, 512], F32, kind="ExternalInput").ap()
    At_d = nc.dram_tensor("At", [128, 32, 128], F32, kind="ExternalInput").ap()
    Bs_d = nc.dram_tensor("Bs", [128, 4096], F32, kind="ExternalInput").ap()
    lw_d = nc.dram_tensor("lw", [128, 4096], F32, kind="ExternalInput").ap()
    bias_d = nc.dram_tensor("bias", [1, 4096], F32, kind="ExternalInput").ap()
    mk0_d = nc.dram_tensor("mk0", [128, 4, 5, 128], BF16, kind="ExternalInput").ap()
    mki_d = nc.dram_tensor("mki", [128, 2, 128], BF16, kind="ExternalInput").ap()
    cst_d = nc.dram_tensor("cst", [128, 256], BF16, kind="ExternalInput").ap()
    out_d = nc.dram_tensor("out", [TCORE, D], F32, kind="ExternalOutput").ap()
    wcache = nc.dram_tensor("wcache", [113, 128, 4096], BF16, kind="Internal").ap()

    dbg = {}
    if DBG:
        for name, shape, dt in [
            ("d_ln1T", [128, 32 * 512], BF16), ("d_kT", [128, 8 * 1024], BF16),
            ("d_v", [128, 8 * 1032], BF16), ("d_qT", [128, 16384], BF16),
            ("d_osb", [128, 4096], BF16), ("d_oT", [128, 32 * 512], BF16),
            ("d_delta", [128, 16384], BF16), ("d_ln2T", [128, 32 * 512], BF16),
            ("d_hT", [128, 512], BF16),
            ("d_hln", [128, 32 * 512], BF16), ("d_kT0", [128, 8 * 1024], BF16),
            ("d_bs", [128, 4096], BF16), ("d_brow", [128, 4096], BF16),
            ("d_bs2", [128, 4096], BF16), ("d_xsb0", [128, 4096], F32),
            ("d_xsb1", [128, 4096], F32), ("d_pT", [128, 640], BF16),
            ("d_S", [128, 2], F32), ("d_sma", [128, 640], F32),
            ("d_p7", [128, 2048], F32),
        ]:
            dbg[name] = nc.dram_tensor(name, shape, dt, kind="ExternalOutput").ap()

    with ExitStack() as stack:
        ent = stack.enter_context

        A1 = ent(nc.sbuf_tensor("A1", [128, 32, 512], BF16))
        A2 = ent(nc.sbuf_tensor("A2", [128, 16384], BF16))
        kT = ent(nc.sbuf_tensor("kT", [128, 8, 1024], BF16))
        vR = ent(nc.sbuf_tensor("vR", [128, 8, 1032], BF16))
        wbuf = ent(nc.sbuf_tensor("wbuf", [128, 12288], BF16))
        xsb = ent(nc.sbuf_tensor("xsb", [128, 2, 4096], F32))
        lnt = ent(nc.sbuf_tensor("lnt", [128, 2, 4096], BF16))
        osb = ent(nc.sbuf_tensor("osb", [128, 4096], BF16))
        pTs = ent(nc.sbuf_tensor("pTs", [128, 3, 640], BF16))
        sqs = ent(nc.sbuf_tensor("sqs", [128, 2048], BF16))
        hT = ent(nc.sbuf_tensor("hT", [128, 512], BF16))
        Bs16 = ent(nc.sbuf_tensor("Bs16", [128, 4096], BF16))
        mk0 = ent(nc.sbuf_tensor("mk0s", [128, 4, 5, 128], BF16))
        mki = ent(nc.sbuf_tensor("mkis", [128, 2, 128], BF16))
        cst = ent(nc.sbuf_tensor("csts", [128, 256], BF16))
        xstr = ent(nc.sbuf_tensor("xstr", [128, 4, 512], F32))
        st = ent(nc.sbuf_tensor("st", [128, 16], F32))
        invS = ent(nc.sbuf_tensor("invS", [128, 2], F32))

        ident = cst[:, 0:128]
        ones_col = cst[:, 128:129]
        ones_row = cst[0:1, 128:256]

        qTv = A2[:].rearrange("p (a b) -> p a b", a=32)
        dv = A2[:].rearrange("p (a b) -> p a b", a=4)

        pA = ent(nc.psum_tensor("pA", [128, 1024], F32))
        pB = ent(nc.psum_tensor("pB", [128, 1024], F32))
        pC = ent(nc.psum_tensor("pC", [128, 1024], F32))
        pD = ent(nc.psum_tensor("pD", [128, 2048], BF16))
        pD32 = pD[:].bitcast(F32)

        P512 = {
            "pA0": pA[:, 0:512], "pA1": pA[:, 512:1024],
            "pB0": pB[:, 0:512], "pB1": pB[:, 512:1024],
            "pC0": pC[:, 0:512], "pC1": pC[:, 512:1024],
            "pD0": pD32[:, 0:512], "pD1": pD32[:, 512:1024],
        }
        strip = {0: pA[:, 0:640], 1: pB[:, 0:640], 2: pC[:, 0:640]}
        o_ps = {0: pD32[:, 0:129], 1: pD32[:, 512:641]}
        S_ps = {0: pD32[:, 128:129], 1: pD32[:, 640:641]}
        pD_t = {0: pD[:, 0:1024], 1: pD[:, 1024:2048]}
        pA16 = pA[:].bitcast(BF16)
        pB16 = pB[:].bitcast(BF16)

        bank_free = {k: None for k in P512}
        STRIP_BANKS = {0: ("pA0", "pA1"), 1: ("pB0", "pB1"), 2: ("pC0", "pC1")}
        OS_BANK = {0: "pD0", 1: "pD1"}
        PDT_BANK = {0: "pD0", 1: "pD1"}

        for sname in ("dX", "dW", "dC", "dO", "dWB", "sPE", "sVE", "sSE"):
            S.add_sem(sname, ent(nc.semaphore(sname)))

        wstate = {"i": 0, "consumed": {}, "wb": {}}

        def sched_slab(in_ap3, ncols, uid=None, mode="fp32"):
            idx = wstate["i"]
            wstate["i"] += 1
            half = idx % 3
            if idx >= 3:
                S.wait_lazy("gpsimd", "sPE", wstate["consumed"], idx - 3)
                S.wait_lazy("gpsimd", "dWB", wstate["wb"], idx - 3)
            dst = wbuf[:, half * 4096: half * 4096 + ncols]
            if mode == "bf16":
                srcap = wcache[uid][:, 0:ncols]
            else:
                srcap = in_ap3.rearrange("p a b -> p (a b)")
            tok = S.inc("gpsimd", "dW",
                        lambda e, d=dst, s=srcap: e.dma_start(out=d, in_=s), 16)
            wstate["meta"] = getattr(wstate, "meta", None) or wstate.setdefault(
                "meta", {})
            wstate["meta"][idx] = (uid, half, ncols, mode)
            return idx, half, tok

        def slab_done(idx, tok):
            wstate["consumed"][idx] = tok[1]
            uid, half, ncols, mode = wstate["meta"][idx]
            if mode == "wb":
                S.wait("sync", tok)
                dst = wbuf[:, half * 4096: half * 4096 + ncols]
                tokwb = S.inc("sync", "dWB",
                              lambda e, u=uid, d=dst, ncols=ncols: e.dma_start(
                                  out=wcache[u][:, 0:ncols], in_=d), 16)
                wstate["wb"][idx] = tokwb[1]

        def wview_q(half):
            return wbuf[:, half * 4096: half * 4096 + 4096].rearrange(
                "p (a b) -> p a b", a=32)

        def wview_b8(half):
            return wbuf[:, half * 4096: half * 4096 + 4096].rearrange(
                "p (a b) -> p a b", a=8)

        def pe(fn):
            S.emit("tensor", fn)

        def pe_inc(fn):
            return S.inc("tensor", "sPE", fn)

        def ve_inc(fn):
            return S.inc("vector", "sVE", fn)

        def se_inc(fn):
            return S.inc("scalar", "sSE", fn)

        def wait_banks(eng, keys):
            for k in keys:
                S.wait(eng, bank_free[k])

        def dump(name, src_ap, tok_list):
            if not DBG or name not in dbg:
                return
            for t in tok_list:
                S.wait("sync", t)
            S.inc("sync", "dO",
                  lambda e, d=dbg[name], s=src_ap: e.dma_start(out=d, in_=s), 16)
            for eng in ENGS:
                S.wait(eng, ("dO", S.cnt["dO"]))

        tok_c = []
        tok_c.append(S.inc("sync", "dC", lambda e: e.dma_start(out=mk0[:], in_=mk0_d), 16))
        tok_c.append(S.inc("sync", "dC", lambda e: e.dma_start(out=mki[:], in_=mki_d), 16))
        tok_c.append(S.inc("sync", "dC", lambda e: e.dma_start(out=cst[:], in_=cst_d), 16))
        A2f = A2[:].bitcast(F32)
        tok_lw = S.inc("sync", "dC", lambda e: e.dma_start(
            out=A2f[:, 4096:8192], in_=lw_d), 16)
        tok_c.append(tok_lw)
        tok_bs = S.inc("sync", "dC", lambda e: e.dma_start(
            out=A2f[:, 0:4096], in_=Bs_d), 16)
        tok_c.append(tok_bs)
        tok_const = list(tok_c)
        bsref = {}
        vRh = vR[:].rearrange("p s (h e) -> p s h e", h=8)
        tok_vones = S.inc("vector", "sVE", lambda e: e.memset(
            vRh[:, :, :, 128:129], 1.0))

        def make_bs16():
            S.wait("vector", tok_bs)
            S.wait("vector", tok_lw)
            t = ve_inc(lambda e: e.tensor_tensor(
                A2f[:, 0:4096], A2f[:, 0:4096], A2f[:, 4096:8192], op=ALU.mult))
            S.wait("scalar", t)
            tk = se_inc(lambda e: e.mul(Bs16[:], A2f[:, 0:4096], 2.0))
            bsref["tok"] = tk
            if DBG:
                dump("d_bs2", Bs16[:], [tk])
            return tk

        state = {
            "xsb_free": {0: None, 1: None},
            "lnt_free": {0: None, 1: None},
            "A1_free": None,
            "A2_free": None,
            "attn_done": None,
            "qT_ready": None,
            "kv_ready": None,
            "pTs_free": {0: None, 1: None, 2: None},
            "invS_free": {0: None, 1: None},
            "osb_free": None,
            "xstr_free": {0: None, 1: None, 2: None, 3: None},
            "strip_free": {0: None, 1: None, 2: None},
            "oS_free": {0: None, 1: None},
            "hT_ready": None,
        }
        pdt_state = {"par": 0}

        K_SX, K_SQ, K_MEAN, K_VAR, K_T1, K_RSTD, K_NEGMR, K_X2 = range(8)

        def stc(b, k):
            return st[:, 8 * b + k: 8 * b + k + 1]

        def x_preload(row0, b):
            S.wait("sync", state["xsb_free"][b])
            tok = S.inc("sync", "dX", lambda e, r=row0, b=b: e.dma_start(
                out=xsb[:, b, :], in_=xt[r:r + 128, :]), 16)
            state["xsb_free"][b] = ("LOADED", tok)
            return tok

        def ln_stats(row0, b, add_delta_tb=None, delta_ready=None, tokx=None):
            if tokx is None:
                S.wait("sync", state["xsb_free"][b])
                tokx = S.inc("sync", "dX", lambda e, r=row0, b=b: e.dma_start(
                    out=xsb[:, b, :], in_=xt[r:r + 128, :]), 16)
            H = D // 2
            if add_delta_tb is not None:
                S.wait("vector", tokx)
                S.wait("vector", delta_ready)
                ta1 = ve_inc(lambda e, b=b, tb=add_delta_tb: e.tensor_tensor(
                    xsb[:, b, 0:H], xsb[:, b, 0:H], dv[:, tb, 0:H], op=ALU.add))
                ta2 = ve_inc(lambda e, b=b, tb=add_delta_tb: e.tensor_tensor(
                    xsb[:, b, H:D], xsb[:, b, H:D], dv[:, tb, H:D], op=ALU.add))
            else:
                ta1 = ta2 = tokx
            S.wait("scalar", ta1)
            tsq1 = se_inc(lambda e, b=b: e.activation(
                sqs[:], xsb[:, b, 0:H], AF.Square,
                accum_out=stc(b, K_SQ)))
            S.wait("scalar", ta2)
            tsq2 = se_inc(lambda e, b=b: e.activation(
                sqs[:], xsb[:, b, H:D], AF.Square,
                accum_out=stc(b, K_VAR)))
            S.wait("vector", tsq1)
            tr1 = ve_inc(lambda e, b=b: e.reduce_sum(
                stc(b, K_SX), xsb[:, b, 0:H], axis=AX.X))
            S.wait("vector", tsq2)
            tr2 = ve_inc(lambda e, b=b: e.reduce_sum(
                stc(b, K_X2), xsb[:, b, H:D], axis=AX.X))
            S.wait("vector", tr1)
            S.wait("vector", tr2)
            t0 = ve_inc(lambda e, b=b: e.tensor_tensor(
                stc(b, K_SX), stc(b, K_SX), stc(b, K_X2), op=ALU.add))
            S.wait("vector", t0)
            t0b = ve_inc(lambda e, b=b: e.tensor_tensor(
                stc(b, K_SQ), stc(b, K_SQ), stc(b, K_VAR), op=ALU.add))
            S.wait("vector", t0b)
            t1 = ve_inc(lambda e, b=b: e.tensor_scalar_mul(
                stc(b, K_MEAN), stc(b, K_SX), 1.0 / D))
            t2 = ve_inc(lambda e, b=b: e.tensor_scalar_mul(
                stc(b, K_VAR), stc(b, K_SQ), 1.0 / D))
            S.wait("vector", t1)
            S.wait("vector", t2)
            t3 = ve_inc(lambda e, b=b: e.tensor_tensor(
                stc(b, K_T1), stc(b, K_MEAN), stc(b, K_MEAN), op=ALU.mult))
            S.wait("vector", t3)
            t4 = ve_inc(lambda e, b=b: e.tensor_scalar(
                stc(b, K_T1), stc(b, K_T1), -1.0, EPS, op0=ALU.mult, op1=ALU.add))
            S.wait("vector", t4)
            tok_var = ve_inc(lambda e, b=b: e.tensor_tensor(
                stc(b, K_VAR), stc(b, K_VAR), stc(b, K_T1), op=ALU.add))
            S.wait("scalar", tok_var)
            tok_ln = se_inc(lambda e, b=b: e.activation(
                stc(b, K_T1), stc(b, K_VAR), AF.Ln))
            S.wait("scalar", tok_ln)
            tok_rstd = se_inc(lambda e, b=b: e.activation(
                stc(b, K_RSTD), stc(b, K_T1), AF.Exp, scale=-0.5))
            S.wait("vector", tok_rstd)
            tok_negmr = ve_inc(lambda e, b=b: e.tensor_scalar(
                stc(b, K_NEGMR), stc(b, K_MEAN), stc(b, K_RSTD), -1.0,
                op0=ALU.mult, op1=ALU.mult))
            return (tok_rstd, tok_negmr)

        def ln_norm(b, tok_stats):
            tok_rstd, tok_negmr = tok_stats
            H = D // 2
            S.wait("scalar", tok_negmr)
            S.wait("scalar", tok_rstd)
            S.wait("scalar", state["lnt_free"][b])
            tn1 = se_inc(lambda e, b=b: e.activation(
                lnt[:, b, 0:H], xsb[:, b, 0:H], AF.Identity,
                scale=stc(b, K_RSTD), bias=stc(b, K_NEGMR)))
            tn2 = se_inc(lambda e, b=b: e.activation(
                lnt[:, b, H:D], xsb[:, b, H:D], AF.Identity,
                scale=stc(b, K_RSTD), bias=stc(b, K_NEGMR)))
            state["xsb_free"][b] = tn2
            return (tn1, tn2)

        def ln_front(row0, b, add_delta_tb=None, delta_ready=None, tokx=None):
            return ln_norm(b, ln_stats(row0, b, add_delta_tb=add_delta_tb,
                                       delta_ready=delta_ready, tokx=tokx))

        def ln_back(b, toff, tok_norm, a1_ready=None):
            tn1, tn2 = tok_norm
            last_ev = None
            tok_tr = None
            for g8 in range(4):
                par = pdt_state["par"]
                pdt_state["par"] = 1 - par
                S.wait("tensor", tn1 if g8 < 2 else tn2)
                if a1_ready is not None and g8 == 0:
                    S.wait("tensor", a1_ready)
                wait_banks("tensor", [PDT_BANK[par]])
                for kk in range(8):
                    kb = g8 * 8 + kk
                    fn = (lambda e, par=par, kk=kk, b=b, kb=kb: e.transpose(
                        pD_t[par][:, kk * 128:(kk + 1) * 128],
                        lnt[:, b, kb * 128:(kb + 1) * 128], ident))
                    if kk == 7:
                        tok_tr = pe_inc(fn)
                    else:
                        pe(fn)
                S.wait("vector", tok_tr)
                if a1_ready is not None and g8 == 0:
                    S.wait("vector", a1_ready)
                tok_ev = ve_inc(lambda e, par=par, g8=g8, toff=toff: e.tensor_copy(
                    A1[:, g8 * 8:(g8 + 1) * 8, toff:toff + 128],
                    pD_t[par][:].rearrange("p (k c) -> p k c", k=8)))
                bank_free[PDT_BANK[par]] = tok_ev
                last_ev = tok_ev
            state["lnt_free"][b] = tok_tr
            return last_ev

        def ln_tile(row0, b, toff, a1_ready=None, add_delta_tb=None,
                    delta_ready=None, tokx=None):
            tok_norm = ln_front(row0, b, add_delta_tb=add_delta_tb,
                                delta_ready=delta_ready, tokx=tokx)
            return ln_back(b, toff, tok_norm, a1_ready=a1_ready)

        def kv_phase(s0, ln_ready, wmode="fp32"):
            pp = ["pA0", "pB0"]
            last_ev = None
            for kd in range(8):
                idx, half, tok_arr = sched_slab(wk_d[kd], 4096, uid=kd,
                                                mode=wmode)
                key = pp[kd % 2]
                S.wait("tensor", tok_arr)
                if kd == 0:
                    S.wait("tensor", ln_ready)
                wait_banks("tensor", [key])
                wv3 = wview_q(half)
                tok_mm = None
                for K in range(32):
                    fn = (lambda e, key=key, wv3=wv3, K=K: e.matmul(
                        P512[key], wv3[:, K, :], A1[:, K, 0:512],
                        start=(K == 0), stop=(K == 31)))
                    if K == 31:
                        tok_mm = pe_inc(fn)
                    else:
                        pe(fn)
                slab_done(idx, tok_mm)
                S.wait("vector", tok_mm)
                S.wait("vector", state["attn_done"])
                tok_ev = ve_inc(lambda e, kd=kd, s0=s0, key=key: e.tensor_copy(
                    kT[:, kd, s0 * 128: s0 * 128 + 512], P512[key]))
                bank_free[key] = tok_ev
                last_ev = tok_ev
            vkeys = ["pC0", "pC1", "pD0", "pD1"]
            for vd in range(2):
                tok_mm = None
                for h2 in range(4):
                    idx, half, tok_arr = sched_slab(
                        wv_d[vd][:, 8 * h2:8 * h2 + 8, :], 4096,
                        uid=8 + vd * 4 + h2, mode=wmode)
                    S.wait("tensor", tok_arr)
                    if h2 == 0:
                        wait_banks("tensor", vkeys)
                    wv3 = wview_b8(half)
                    for kk in range(8):
                        K = 8 * h2 + kk
                        for tb in range(4):
                            fn = (lambda e, tb=tb, wv3=wv3, kk=kk, K=K: e.matmul(
                                P512[vkeys[tb]], A1[:, K, tb * 128:(tb + 1) * 128],
                                wv3[:, kk, :],
                                start=(K == 0), stop=(K == 31)))
                            if kk == 7 and tb == 3:
                                tok_mm = pe_inc(fn)
                            else:
                                pe(fn)
                    slab_done(idx, tok_mm)
                S.wait("vector", tok_mm)
                S.wait("vector", state["attn_done"])
                for tb in range(4):
                    slot = (s0 + tb) % 8
                    tok_ev = ve_inc(lambda e, tb=tb, slot=slot, vd=vd: e.tensor_copy(
                        vRh[:, slot, vd * 4:(vd + 1) * 4, 0:128],
                        P512[vkeys[tb]].rearrange("p (h e) -> p h e", h=4)))
                    bank_free[vkeys[tb]] = tok_ev
                    last_ev = tok_ev
            return last_ev

        def q_phase(ln_ready, wmode="fp32"):
            pp = ["pA0", "pB0"]
            last_ev = None
            for qb in range(32):
                idx, half, tok_arr = sched_slab(wq_d[qb], 4096, uid=16 + qb,
                                                mode=wmode)
                key = pp[qb % 2]
                S.wait("tensor", tok_arr)
                if qb == 0:
                    S.wait("tensor", ln_ready)
                wait_banks("tensor", [key])
                wv3 = wview_q(half)
                tok_mm = None
                for K in range(32):
                    fn = (lambda e, key=key, wv3=wv3, K=K: e.matmul(
                        P512[key], wv3[:, K, :], A1[:, K, 0:512],
                        start=(K == 0), stop=(K == 31)))
                    if K == 31:
                        tok_mm = pe_inc(fn)
                    else:
                        pe(fn)
                slab_done(idx, tok_mm)
                S.wait("vector", tok_mm)
                if qb == 0:
                    S.wait("vector", state["A2_free"])
                tok_ev = ve_inc(lambda e, qb=qb, key=key: e.tensor_copy(
                    qTv[:, qb, :], P512[key]))
                bank_free[key] = tok_ev
                last_ev = tok_ev
            return last_ev

        def attn_phase(c):
            tok_osb_tr = None
            last_tr_ev = None
            box = {"tok_pv_final": None, "tok_last_oev": None,
                   "first_evac_done": False}
            toks_exp = {}

            def emit_qk(a, i):
                par = i % 3
                sk0, sk1 = STRIP_BANKS[par]
                S.wait("tensor", state["strip_free"][par])
                wait_banks("tensor", [sk0, sk1])
                if a == 0 and i == 0:
                    S.wait("tensor", state["qT_ready"])
                    S.wait("tensor", state["kv_ready"])
                kvh = i // 4
                hq = i
                tok_qk = None
                for j in range(5):
                    slot = (4 * c + a + j) % 8
                    fn = (lambda e, par=par, j=j, slot=slot, kvh=kvh, hq=hq,
                          a=a: e.matmul(
                              strip[par][:, j * 128:(j + 1) * 128],
                              kT[:, kvh, slot * 128:(slot + 1) * 128],
                              qTv[:, hq, a * 128:(a + 1) * 128],
                              start=True, stop=True))
                    if j == 4:
                        tok_qk = pe_inc(fn)
                    else:
                        pe(fn)
                S.wait("vector", tok_qk)
                if c == 0:
                    tok_mask = None
                    for j in range(5):
                        tok_mask = ve_inc(lambda e, par=par, j=j, a=a:
                                          e.tensor_tensor(
                            strip[par][:, j * 128:(j + 1) * 128],
                            strip[par][:, j * 128:(j + 1) * 128],
                            mk0[:, a, j, :], op=ALU.add))
                else:
                    ve_inc(lambda e, par=par: e.tensor_tensor(
                        strip[par][:, 0:128], strip[par][:, 0:128],
                        mki[:, 0, :], op=ALU.add))
                    tok_mask = ve_inc(lambda e, par=par: e.tensor_tensor(
                        strip[par][:, 512:640], strip[par][:, 512:640],
                        mki[:, 1, :], op=ALU.add))
                S.wait("scalar", tok_mask)
                S.wait("scalar", state["pTs_free"][par])
                tok_exp = se_inc(lambda e, par=par: e.activation(
                    pTs[:, par, :], strip[par][:, 0:640], AF.Exp,
                    scale=ASCALE))
                state["strip_free"][par] = tok_exp
                bank_free[sk0] = tok_exp
                bank_free[sk1] = tok_exp
                toks_exp[i] = tok_exp

            def emit_pv(a, i):
                par = i % 3
                opar = i % 2
                ob = OS_BANK[opar]
                S.wait("tensor", toks_exp[i])
                S.wait("tensor", state["oS_free"][opar])
                wait_banks("tensor", [ob])
                kvh = i // 4
                hq = i
                tok_pv = None
                for j in range(5):
                    slot = (4 * c + a + j) % 8
                    fn = (lambda e, par=par, opar=opar, j=j, slot=slot, kvh=kvh:
                          e.matmul(
                              o_ps[opar], pTs[:, par, j * 128:(j + 1) * 128],
                              vR[:, slot, kvh * 129:kvh * 129 + 129],
                              start=(j == 0), stop=(j == 4)))
                    if j == 4:
                        tok_pv = pe_inc(fn)
                    else:
                        pe(fn)
                state["pTs_free"][par] = tok_pv
                S.wait("vector", tok_pv)
                S.wait("vector", state["invS_free"][opar])
                tok_inv = ve_inc(lambda e, opar=opar: e.reciprocal(
                    invS[:, opar:opar + 1], S_ps[opar]))
                S.wait("scalar", tok_inv)
                if a == 0 and i == 0:
                    S.wait("scalar", state["osb_free"])
                if not box["first_evac_done"] and tok_osb_tr is not None:
                    S.wait("scalar", tok_osb_tr)
                box["first_evac_done"] = True
                tok_oev = se_inc(lambda e, opar=opar, hq=hq: e.activation(
                    osb[:, hq * 128:(hq + 1) * 128],
                    o_ps[opar][:, 0:128],
                    AF.Identity, scale=invS[:, opar:opar + 1]))
                state["invS_free"][opar] = tok_oev
                state["oS_free"][opar] = tok_oev
                bank_free[ob] = tok_oev
                box["tok_last_oev"] = tok_oev
                box["tok_pv_final"] = tok_pv

            tr_areas = [(pA16, 0, "pA0"), (pA16, 1024, "pA1"),
                        (pB16, 0, "pB0"), (pB16, 1024, "pB1")]
            for a in range(4):
                box["first_evac_done"] = False
                for i in range(32):
                    emit_qk(a, i)
                    if i >= 2:
                        emit_pv(a, i - 2)
                emit_pv(a, 30)
                emit_pv(a, 31)
                tok_last_oev = box["tok_last_oev"]
                tok_tr = None
                for g8 in range(4):
                    buf16, off, bkey = tr_areas[g8]
                    S.wait("tensor", tok_last_oev)
                    if a == 0 and g8 == 0:
                        S.wait("tensor", state["A1_free"])
                    wait_banks("tensor", [bkey])
                    for kk in range(8):
                        kb = g8 * 8 + kk
                        fn = (lambda e, buf16=buf16, off=off, kk=kk, kb=kb:
                              e.transpose(
                                  buf16[:, off + kk * 128: off + (kk + 1) * 128],
                                  osb[:, kb * 128:(kb + 1) * 128], ident))
                        if kk == 7:
                            tok_tr = pe_inc(fn)
                        else:
                            pe(fn)
                    S.wait("vector", tok_tr)
                    if a == 0 and g8 == 0:
                        S.wait("vector", state["A1_free"])
                    tok_ev = ve_inc(lambda e, buf16=buf16, off=off, g8=g8, a=a:
                                    e.tensor_copy(
                        A1[:, g8 * 8:(g8 + 1) * 8, a * 128:(a + 1) * 128],
                        buf16[:, off:off + 1024].rearrange(
                            "p (k c) -> p k c", k=8)))
                    bank_free[bkey] = tok_ev
                    last_tr_ev = tok_ev
                tok_osb_tr = tok_tr
            state["osb_free"] = tok_osb_tr
            state["attn_done"] = box["tok_pv_final"]
            return last_tr_ev, box["tok_pv_final"]

        def wo_phase(oT_ready, qT_done, wmode="fp32"):
            groups = [["pA0", "pA1", "pB0", "pB1"], ["pC0", "pC1", "pD0", "pD1"]]
            last_ev = None
            for og in range(8):
                keys = groups[og % 2]
                tok_mm = None
                for h2 in range(4):
                    idx, half, tok_arr = sched_slab(
                        wo_d[og][:, 8 * h2:8 * h2 + 8, :], 4096,
                        uid=48 + og * 4 + h2, mode=wmode)
                    S.wait("tensor", tok_arr)
                    if og == 0 and h2 == 0:
                        S.wait("tensor", oT_ready)
                    if h2 == 0:
                        wait_banks("tensor", keys)
                    wv3 = wview_b8(half)
                    for kk in range(8):
                        K = 8 * h2 + kk
                        for tb in range(4):
                            fn = (lambda e, tb=tb, kk=kk, K=K, wv3=wv3,
                                  keys=keys: e.matmul(
                                      P512[keys[tb]],
                                      A1[:, K, tb * 128:(tb + 1) * 128],
                                      wv3[:, kk, :],
                                      start=(K == 0), stop=(K == 31)))
                            if kk == 7 and tb == 3:
                                tok_mm = pe_inc(fn)
                            else:
                                pe(fn)
                    slab_done(idx, tok_mm)
                S.wait("vector", tok_mm)
                if og == 0:
                    S.wait("vector", qT_done)
                for tb in range(4):
                    tok_ev = ve_inc(lambda e, tb=tb, og=og, keys=keys: e.tensor_copy(
                        dv[:, tb, og * 512:(og + 1) * 512], P512[keys[tb]]))
                    bank_free[keys[tb]] = tok_ev
                    last_ev = tok_ev
            return last_ev

        def ffn_preload(wmode="fp32"):
            S.wait("gpsimd", state["osb_free"])
            tok_bias = S.inc("gpsimd", "dW", lambda e: e.dma_start(
                out=osb[0:1, :], in_=bias_d), 16)
            aslab = sched_slab(At_d, 4096, uid=80, mode=wmode)
            slabs = []
            for og in range(8):
                for h2 in range(4):
                    slabs.append(sched_slab(
                        Wt_d[og][:, 8 * h2:8 * h2 + 8, :], 4096,
                        uid=81 + og * 4 + h2, mode=wmode))
            return tok_bias, aslab, slabs

        def ffn_phase(c, ln2b, delta_ready, preload):
            tok_bias, aslab, slabs = preload
            aidx, ahalf, tok_aarr = aslab
            av3 = wview_q(ahalf)
            tok_hev = None

            groups = [["pA0", "pA1", "pB0", "pB1"], ["pC0", "pC1", "pD0", "pD1"]]
            stripe_i = 0
            tok_lora_last = None
            last_out = None
            for og in range(8):
                keys = groups[og % 2]
                for h2 in range(4):
                    idx, half, tok_arr = slabs[og * 4 + h2]
                    S.wait("tensor", tok_arr)
                    if h2 == 0:
                        wait_banks("tensor", keys)
                    wv3 = wview_b8(half)
                    tok_mm = None
                    for kk in range(8):
                        K = 8 * h2 + kk
                        for tb in range(4):
                            if og == 0 and h2 == 0 and kk == 0:
                                S.wait("tensor", ln2b[tb])
                            fn = (lambda e, tb=tb, kk=kk, K=K, wv3=wv3,
                                  keys=keys: e.matmul(
                                      P512[keys[tb]],
                                      A1[:, K, tb * 128:(tb + 1) * 128],
                                      wv3[:, kk, :],
                                      start=(K == 0), stop=False))
                            if kk == 7 and tb == 3:
                                tok_mm = pe_inc(fn)
                            else:
                                pe(fn)
                    slab_done(idx, tok_mm)
                    if og == 0 and h2 == 0:
                        S.wait("tensor", tok_aarr)
                        wait_banks("tensor", ["pC0"])
                        tok_h = None
                        for K in range(32):
                            fn = (lambda e, K=K, av3=av3: e.matmul(
                                P512["pC0"], av3[:, K, :], A1[:, K, 0:512],
                                start=(K == 0), stop=(K == 31)))
                            if K == 31:
                                tok_h = pe_inc(fn)
                            else:
                                pe(fn)
                        slab_done(aidx, tok_h)
                        S.wait("vector", tok_h)
                        S.wait("vector", state["hT_ready"])
                        tok_hev = ve_inc(lambda e: e.tensor_copy(
                            hT[:], P512["pC0"]))
                        bank_free["pC0"] = tok_hev
                        if DBG and c == DBGC:
                            dump("d_hT", hT[:], [tok_hev])
                if not NOBIAS:
                    if og == 0:
                        S.wait("tensor", tok_bias)
                    for tb in range(4):
                        pe(lambda e, tb=tb, og=og, keys=keys: e.matmul(
                            P512[keys[tb]], ones_row,
                            osb[0:1, og * 512:(og + 1) * 512],
                            start=False, stop=False))
                if og == 0:
                    S.wait("tensor", delta_ready)
                for tb in range(4):
                    pe(lambda e, tb=tb, og=og, keys=keys: e.matmul(
                        P512[keys[tb]], ident,
                        dv[:, tb, og * 512:(og + 1) * 512],
                        start=False, stop=False))
                tok_grp = None
                if og == 0:
                    S.wait("tensor", tok_hev)
                for tb in range(4):
                    fn = (lambda e, tb=tb, og=og, keys=keys: e.matmul(
                        P512[keys[tb]], hT[:, tb * 128:(tb + 1) * 128],
                        Bs16[:, og * 512:(og + 1) * 512],
                        start=False, stop=True))
                    if tb == 3:
                        tok_grp = pe_inc(fn)
                    else:
                        pe(fn)
                tok_lora_last = tok_grp
                if DBG and c == DBGC and og == 0:
                    lnt32 = lnt[:, 0, :].bitcast(F32)
                    tks = []
                    for tb in range(4):
                        S.wait("vector", tok_grp)
                        tks.append(ve_inc(lambda e, tb=tb, keys=keys, lnt32=lnt32:
                                          e.tensor_copy(lnt32[:, tb * 512:(tb + 1) * 512],
                                                        P512[keys[tb]])))
                    dump("d_p7", lnt32, tks)
                    dump("d_bs", Bs16[:], [])
                    dump("d_brow", osb[:], [])
                for tb in range(4):
                    si = stripe_i % 4
                    stripe_i += 1
                    row0 = HALO + c * CH + tb * 128
                    S.wait("sync", state["xstr_free"][si])
                    tokx = S.inc("sync", "dX",
                                 lambda e, si=si, row0=row0, og=og: e.dma_start(
                                     out=xstr[:, si, :],
                                     in_=xt[row0:row0 + 128,
                                            og * 512:(og + 1) * 512]), 16)
                    S.wait("vector", tokx)
                    S.wait("vector", tok_grp)
                    tok_st = ve_inc(lambda e, si=si, tb=tb, keys=keys: e.tensor_tensor(
                        xstr[:, si, :], xstr[:, si, :], P512[keys[tb]],
                        op=ALU.add))
                    bank_free[keys[tb]] = tok_st
                    S.wait("sync", tok_st)
                    row0o = c * CH + tb * 128
                    tok_out = S.inc("sync", "dO",
                                    lambda e, si=si, row0o=row0o, og=og:
                                    e.dma_start(
                                        out=out_d[row0o:row0o + 128,
                                                  og * 512:(og + 1) * 512],
                                        in_=xstr[:, si, :]), 16)
                    state["xstr_free"][si] = tok_out
                    last_out = tok_out
            state["hT_ready"] = tok_lora_last
            state["osb_free"] = tok_lora_last
            return last_out, tok_lora_last

        for tok in tok_const:
            S.wait("tensor", tok)
            S.wait("vector", tok)
            S.wait("scalar", tok)

        last_ev = None
        for hb in range(4):
            last_ev = ln_tile(hb * 128, hb % 2, hb * 128)
        if DBG:
            dump("d_hln", A1[:].rearrange("p a b -> p (a b)"), [last_ev])
        tok_kv = kv_phase(0, last_ev)
        state["kv_ready"] = tok_kv
        state["A1_free"] = tok_kv
        tok_bs16 = make_bs16()
        state["A2_free"] = tok_bs16
        for eng in ("tensor",):
            S.wait(eng, tok_bs16)
        if DBG:
            dump("d_kT0", kT[:].rearrange("p a b -> p (a b)"), [tok_kv])

        for c in range(NCHUNK):
            if DBG and c == DBGC + 1:
                for eng in ENGS:
                    S.wait(eng, ("dO", S.cnt["dO"]))
            last_ev = None
            if c == 0:
                for tb in range(4):
                    a1r = state["A1_free"] if tb == 0 else None
                    last_ev = ln_tile(HALO + c * CH + tb * 128, tb % 2,
                                      tb * 128, a1_ready=a1r)
            else:
                ln_back(0, 0, frontsP1[0], a1_ready=state["A1_free"])
                fr2 = ln_norm(0, statsP1[0])
                ln_back(1, 128, frontsP1[1])
                fr3 = ln_norm(1, statsP1[1])
                ln_back(0, 256, fr2)
                last_ev = ln_back(1, 384, fr3)
            ln_ready = last_ev
            if DBG and c == DBGC:
                dump("d_ln1T", A1[:].rearrange("p a b -> p (a b)"), [ln_ready])
            s0 = (4 + 4 * c) % 8
            tok_kv = kv_phase(s0, ln_ready,
                              wmode=("wb" if c == 0 else "bf16"))
            state["kv_ready"] = tok_kv
            tok_q = q_phase(ln_ready, wmode=("wb" if c == 0 else "bf16"))
            state["qT_ready"] = tok_q
            if DBG and c == DBGC:
                dump("d_kT", kT[:].rearrange("p a b -> p (a b)"), [tok_kv])
                dump("d_v", vR[:].rearrange("p a b -> p (a b)"), [tok_kv])
                dump("d_qT", A2[:, 0:16384], [tok_q])
            state["A1_free"] = tok_q
            oT_ready, tok_attn = attn_phase(c)
            if DBG and c == DBGC:
                dump("d_osb", osb[:], [state["oS_free"][0], state["oS_free"][1]])
                dump("d_oT", A1[:].rearrange("p a b -> p (a b)"), [oT_ready])
            tokx6 = [x_preload(HALO + c * CH + tb * 128, tb % 2)
                     for tb in range(2)] + [None, None]
            tok_delta = wo_phase(oT_ready, tok_attn,
                                 wmode=("wb" if c == 0 else "bf16"))
            if DBG and c == DBGC:
                dump("d_delta", A2[:, 0:16384], [tok_delta])
            state["A1_free"] = tok_delta
            preload = ffn_preload(wmode=("wb" if c == 0 else "bf16"))
            ln2b = []
            for tb in range(4):
                a1r = state["A1_free"] if tb == 0 else None
                ln2b.append(ln_tile(HALO + c * CH + tb * 128, tb % 2, tb * 128,
                                    a1_ready=a1r, add_delta_tb=tb,
                                    delta_ready=tok_delta, tokx=tokx6[tb]))
            ln2_ready = ln2b[-1]
            if DBG and c == DBGC:
                dump("d_ln2T", A1[:].rearrange("p a b -> p (a b)"), [ln2_ready])
            if c + 1 < NCHUNK:
                fr01 = []
                for tb in range(2):
                    tkx = x_preload(HALO + (c + 1) * CH + tb * 128, tb % 2)
                    fr01.append(ln_front(HALO + (c + 1) * CH + tb * 128,
                                         tb % 2, tokx=tkx))
                st23 = []
                for tb in (2, 3):
                    tkx = x_preload(HALO + (c + 1) * CH + tb * 128, tb % 2)
                    st23.append(ln_stats(HALO + (c + 1) * CH + tb * 128,
                                         tb % 2, tokx=tkx))
            else:
                fr01 = None
                st23 = None
            tok_out, tok_lora = ffn_phase(c, ln2b, tok_delta, preload)
            state["A1_free"] = tok_lora
            state["A2_free"] = tok_out
            frontsP1 = fr01
            statsP1 = st23

        S.wait("sync", ("dO", S.cnt["dO"]))

        with nc.Block() as block:
            @block.sync
            def _(e):
                for f in S.q["sync"]:
                    f(e)

            @block.gpsimd
            def _(e):
                for f in S.q["gpsimd"]:
                    f(e)

            @block.vector
            def _(e):
                for f in S.q["vector"]:
                    f(e)

            @block.scalar
            def _(e):
                for f in S.q["scalar"]:
                    f(e)

            @block.tensor
            def _(e):
                for f in S.q["tensor"]:
                    f(e)

    return nc



def _prep_shared(wq, wk, wv, wo, A, Bmat, W, bias, lw):
    d = {}
    d["wq"] = np.ascontiguousarray(
        wq.reshape(32, 128, 32, 128).transpose(0, 3, 2, 1))
    d["wk"] = np.ascontiguousarray(
        wk.reshape(8, 128, 32, 128).transpose(0, 3, 2, 1))
    d["wv"] = np.ascontiguousarray(
        wv.reshape(2, 512, 32, 128).transpose(0, 3, 2, 1))
    d["wo"] = np.ascontiguousarray(
        wo.reshape(8, 512, 32, 128).transpose(0, 3, 2, 1))
    d["Wt"] = np.ascontiguousarray(
        W.reshape(8, 512, 32, 128).transpose(0, 3, 2, 1))
    acat = A.reshape(128, 4096)
    d["At"] = np.ascontiguousarray(acat.T.reshape(32, 128, 128).transpose(1, 0, 2))
    d["Bs"] = np.ascontiguousarray(Bmat.transpose(0, 2, 1).reshape(128, 4096))
    d["lw"] = np.ascontiguousarray(np.broadcast_to(
        np.repeat(lw, 16).astype(np.float32).reshape(128, 1), (128, 4096)))
    d["bias"] = np.ascontiguousarray(bias.reshape(1, 4096))
    cstv = np.zeros((128, 256), dtype=BF)
    cstv[:, 0:128] = np.eye(128, dtype=BF)
    cstv[:, 128:256] = np.ones((128, 128), dtype=BF)
    d["cst"] = cstv
    return d


def _masks(h):
    cc = np.arange(128)[:, None]
    aa = np.arange(128)[None, :]
    edge = np.where(cc >= aa + 1, 0.0, NEG).astype(np.float32)
    diag = np.where(cc <= aa, 0.0, NEG).astype(np.float32)
    full = np.full((128, 128), NEG, dtype=np.float32)
    none = np.zeros((128, 128), dtype=np.float32)
    mki = np.stack([edge, diag], axis=1).astype(BF)
    mk0 = np.zeros((128, 4, 5, 128), dtype=np.float32)
    for a in range(4):
        for j in range(5):
            if h == 1:
                m = edge if j == 0 else (diag if j == 4 else none)
            else:
                if j == 4:
                    m = diag
                elif j <= 3 - a:
                    m = full
                else:
                    m = none
            mk0[:, a, j, :] = m
    return mk0.astype(BF), mki


def kernel(L0_attn_x, L0_attn_wq, L0_attn_wk, L0_attn_wv, L0_attn_wo,
           L0_ffn_lora_A, L0_ffn_lora_B, L0_ffn_W, L0_ffn_bias,
           L0_ffn_lora_weights):
    x = np.asarray(L0_attn_x, dtype=np.float32)
    shared = _prep_shared(np.asarray(L0_attn_wq, np.float32),
                          np.asarray(L0_attn_wk, np.float32),
                          np.asarray(L0_attn_wv, np.float32),
                          np.asarray(L0_attn_wo, np.float32),
                          np.asarray(L0_ffn_lora_A, np.float32),
                          np.asarray(L0_ffn_lora_B, np.float32),
                          np.asarray(L0_ffn_W, np.float32),
                          np.asarray(L0_ffn_bias, np.float32),
                          np.asarray(L0_ffn_lora_weights, np.float32))
    mk0_h0, mki = _masks(0)
    mk0_h1, _ = _masks(1)

    in_maps = []
    for core in range(8):
        b, h = core // 2, core % 2
        t0 = h * TCORE
        if h == 0:
            halo = np.zeros((HALO, D), dtype=np.float32)
        else:
            halo = x[b, t0 - HALO:t0]
        xtok = np.ascontiguousarray(
            np.concatenate([halo, x[b, t0:t0 + TCORE]], axis=0))
        m = dict(shared)
        m["xt"] = xtok
        m["mk0"] = mk0_h0 if h == 0 else mk0_h1
        m["mki"] = mki
        in_maps.append(m)

    nc = build_nc()
    res = run_bass_kernel_spmd(nc, in_maps, core_ids=list(range(8)))

    outf = np.empty((4, 4096, 4096), dtype=np.float32)
    for core in range(8):
        b, h = core // 2, core % 2
        outf[b, h * TCORE:(h + 1) * TCORE] = res.results[core]["out"]
    return outf




# version 13 (final)
# speedup vs baseline: 1.1364x; 1.1364x over previous
import os
import numpy as np
import ml_dtypes
import concourse.bass as bass
import concourse.mybir as mybir
from concourse.bass_utils import run_bass_kernel_spmd
from contextlib import ExitStack

BF = ml_dtypes.bfloat16
F32 = mybir.dt.float32
BF16 = mybir.dt.bfloat16
AF = mybir.ActivationFunctionType
ALU = mybir.AluOpType
AX = mybir.AxisListType

D = 4096
TCORE = 2048
CH = 512
NCHUNK = int(os.environ.get("MK_NCHUNK", "4"))
HALO = 512
NQ, NKV, HD, GQ = 32, 8, 128, 4
ASCALE = float(1.0 / np.sqrt(np.float32(128.0)))
NEG = -30000.0
EPS = 1e-5
DBG = int(os.environ.get("MK_DBG", "0"))
DBGC = int(os.environ.get("MK_DBGC", "0"))
NOBIAS = int(os.environ.get("MK_NOBIAS", "0"))
XLOAD_SCALAR = int(os.environ.get("MK_XQ", "1"))

ENGS = ("sync", "gpsimd", "vector", "scalar", "tensor")


class Sched:
    def __init__(self):
        self.q = {e: [] for e in ENGS}
        self.sem = {}
        self.cnt = {}
        self.trace = []

    def add_sem(self, name, handle):
        self.sem[name] = handle
        self.cnt[name] = 0

    def wait(self, eng, tok):
        if tok is None:
            return
        if tok[0] == "LOADED":
            tok = tok[1]
        sname, val = tok
        if val <= 0:
            return
        h = self.sem[sname]
        self.trace.append((eng, "wait", sname, val, None))
        self.q[eng].append(lambda e, h=h, v=val: e.wait_ge(h, v))

    def wait_lazy(self, eng, sname, cell, key):
        h = self.sem[sname]
        self.trace.append((eng, "waitlazy", sname, cell, key))

        def f(e, h=h, cell=cell, key=key):
            v = cell.get(key, 0)
            if v > 0:
                e.wait_ge(h, v)

        self.q[eng].append(f)

    def inc(self, eng, sname, fn, n=1):
        h = self.sem[sname]
        self.cnt[sname] += n
        tok = self.cnt[sname]
        self.trace.append((eng, "inc", sname, n, None))
        self.q[eng].append(lambda e, fn=fn, h=h, n=n: fn(e).then_inc(h, n))
        return (sname, tok)

    def emit(self, eng, fn):
        self.trace.append((eng, "op", None, None, None))
        self.q[eng].append(fn)


LAST_SCHED = [None]


def build_nc():
    nc = bass.Bass("TRN2", target_bir_lowering=False)
    S = Sched()
    LAST_SCHED[0] = S

    xt = nc.dram_tensor("xt", [HALO + TCORE, D], F32, kind="ExternalInput").ap()
    wq_d = nc.dram_tensor("wq", [32, 128, 32, 128], F32, kind="ExternalInput").ap()
    wk_d = nc.dram_tensor("wk", [8, 128, 32, 128], F32, kind="ExternalInput").ap()
    wv_d = nc.dram_tensor("wv", [2, 128, 32, 512], F32, kind="ExternalInput").ap()
    wo_d = nc.dram_tensor("wo", [8, 128, 32, 512], F32, kind="ExternalInput").ap()
    Wt_d = nc.dram_tensor("Wt", [8, 128, 32, 512], F32, kind="ExternalInput").ap()
    At_d = nc.dram_tensor("At", [128, 32, 128], F32, kind="ExternalInput").ap()
    Bs_d = nc.dram_tensor("Bs", [128, 4096], F32, kind="ExternalInput").ap()
    lw_d = nc.dram_tensor("lw", [128, 4096], F32, kind="ExternalInput").ap()
    bias_d = nc.dram_tensor("bias", [1, 4096], F32, kind="ExternalInput").ap()
    mk0_d = nc.dram_tensor("mk0", [128, 4, 5, 128], BF16, kind="ExternalInput").ap()
    mki_d = nc.dram_tensor("mki", [128, 2, 128], BF16, kind="ExternalInput").ap()
    cst_d = nc.dram_tensor("cst", [128, 256], BF16, kind="ExternalInput").ap()
    out_d = nc.dram_tensor("out", [TCORE, D], F32, kind="ExternalOutput").ap()
    wcache = nc.dram_tensor("wcache", [113, 128, 4096], BF16, kind="Internal").ap()

    dbg = {}
    if DBG:
        for name, shape, dt in [
            ("d_ln1T", [128, 32 * 512], BF16), ("d_kT", [128, 8 * 1024], BF16),
            ("d_v", [128, 8 * 1032], BF16), ("d_qT", [128, 16384], BF16),
            ("d_osb", [128, 4096], BF16), ("d_oT", [128, 32 * 512], BF16),
            ("d_delta", [128, 16384], BF16), ("d_ln2T", [128, 32 * 512], BF16),
            ("d_hT", [128, 512], BF16),
            ("d_hln", [128, 32 * 512], BF16), ("d_kT0", [128, 8 * 1024], BF16),
            ("d_bs", [128, 4096], BF16), ("d_brow", [128, 4096], BF16),
            ("d_bs2", [128, 4096], BF16), ("d_xsb0", [128, 4096], F32),
            ("d_xsb1", [128, 4096], F32), ("d_pT", [128, 640], BF16),
            ("d_S", [128, 2], F32), ("d_sma", [128, 640], F32),
            ("d_p7", [128, 2048], F32),
        ]:
            dbg[name] = nc.dram_tensor(name, shape, dt, kind="ExternalOutput").ap()

    with ExitStack() as stack:
        ent = stack.enter_context

        A1 = ent(nc.sbuf_tensor("A1", [128, 32, 512], BF16))
        A2 = ent(nc.sbuf_tensor("A2", [128, 16384], BF16))
        kT = ent(nc.sbuf_tensor("kT", [128, 8, 1024], BF16))
        vR = ent(nc.sbuf_tensor("vR", [128, 8, 1032], BF16))
        wbuf = ent(nc.sbuf_tensor("wbuf", [128, 12288], BF16))
        xsb = ent(nc.sbuf_tensor("xsb", [128, 2, 4096], F32))
        lnt = ent(nc.sbuf_tensor("lnt", [128, 2, 4096], BF16))
        osb = ent(nc.sbuf_tensor("osb", [128, 4096], BF16))
        pTs = ent(nc.sbuf_tensor("pTs", [128, 3, 640], BF16))
        sqs = ent(nc.sbuf_tensor("sqs", [128, 2048], BF16))
        hT = ent(nc.sbuf_tensor("hT", [128, 512], BF16))
        Bs16 = ent(nc.sbuf_tensor("Bs16", [128, 4096], BF16))
        mk0 = ent(nc.sbuf_tensor("mk0s", [128, 4, 5, 128], BF16))
        mki = ent(nc.sbuf_tensor("mkis", [128, 2, 128], BF16))
        cst = ent(nc.sbuf_tensor("csts", [128, 256], BF16))
        xstr = ent(nc.sbuf_tensor("xstr", [128, 4, 512], F32))
        st = ent(nc.sbuf_tensor("st", [128, 16], F32))
        invS = ent(nc.sbuf_tensor("invS", [128, 2], F32))

        ident = cst[:, 0:128]
        ones_col = cst[:, 128:129]
        ones_row = cst[0:1, 128:256]

        qTv = A2[:].rearrange("p (a b) -> p a b", a=32)
        dv = A2[:].rearrange("p (a b) -> p a b", a=4)

        pA = ent(nc.psum_tensor("pA", [128, 1024], F32))
        pB = ent(nc.psum_tensor("pB", [128, 1024], F32))
        pC = ent(nc.psum_tensor("pC", [128, 1024], F32))
        pD = ent(nc.psum_tensor("pD", [128, 2048], BF16))
        pD32 = pD[:].bitcast(F32)

        P512 = {
            "pA0": pA[:, 0:512], "pA1": pA[:, 512:1024],
            "pB0": pB[:, 0:512], "pB1": pB[:, 512:1024],
            "pC0": pC[:, 0:512], "pC1": pC[:, 512:1024],
            "pD0": pD32[:, 0:512], "pD1": pD32[:, 512:1024],
        }
        strip = {0: pA[:, 0:640], 1: pB[:, 0:640], 2: pC[:, 0:640]}
        o_ps = {0: pD32[:, 0:129], 1: pD32[:, 512:641]}
        S_ps = {0: pD32[:, 128:129], 1: pD32[:, 640:641]}
        pD_t = {0: pD[:, 0:1024], 1: pD[:, 1024:2048]}
        pA16 = pA[:].bitcast(BF16)
        pB16 = pB[:].bitcast(BF16)

        bank_free = {k: None for k in P512}
        STRIP_BANKS = {0: ("pA0", "pA1"), 1: ("pB0", "pB1"), 2: ("pC0", "pC1")}
        OS_BANK = {0: "pD0", 1: "pD1"}
        PDT_BANK = {0: "pD0", 1: "pD1"}

        for sname in ("dX", "dW", "dC", "dO", "dWB", "sPE", "sVE", "sSE"):
            S.add_sem(sname, ent(nc.semaphore(sname)))

        wstate = {"i": 0, "consumed": {}, "wb": {}}

        def sched_slab(in_ap3, ncols, uid=None, mode="fp32"):
            idx = wstate["i"]
            wstate["i"] += 1
            half = idx % 3
            if idx >= 3:
                S.wait_lazy("gpsimd", "sPE", wstate["consumed"], idx - 3)
                S.wait_lazy("gpsimd", "dWB", wstate["wb"], idx - 3)
            dst = wbuf[:, half * 4096: half * 4096 + ncols]
            if mode == "bf16":
                srcap = wcache[uid][:, 0:ncols]
            else:
                srcap = in_ap3.rearrange("p a b -> p (a b)")
            tok = S.inc("gpsimd", "dW",
                        lambda e, d=dst, s=srcap: e.dma_start(out=d, in_=s), 16)
            wstate["meta"] = getattr(wstate, "meta", None) or wstate.setdefault(
                "meta", {})
            wstate["meta"][idx] = (uid, half, ncols, mode)
            return idx, half, tok

        def slab_done(idx, tok):
            wstate["consumed"][idx] = tok[1]
            uid, half, ncols, mode = wstate["meta"][idx]
            if mode == "wb":
                S.wait("sync", tok)
                dst = wbuf[:, half * 4096: half * 4096 + ncols]
                tokwb = S.inc("sync", "dWB",
                              lambda e, u=uid, d=dst, ncols=ncols: e.dma_start(
                                  out=wcache[u][:, 0:ncols], in_=d), 16)
                wstate["wb"][idx] = tokwb[1]

        def wview_q(half):
            return wbuf[:, half * 4096: half * 4096 + 4096].rearrange(
                "p (a b) -> p a b", a=32)

        def wview_b8(half):
            return wbuf[:, half * 4096: half * 4096 + 4096].rearrange(
                "p (a b) -> p a b", a=8)

        def pe(fn):
            S.emit("tensor", fn)

        def pe_inc(fn):
            return S.inc("tensor", "sPE", fn)

        def ve_inc(fn):
            return S.inc("vector", "sVE", fn)

        def se_inc(fn):
            return S.inc("scalar", "sSE", fn)

        def wait_banks(eng, keys):
            for k in keys:
                S.wait(eng, bank_free[k])

        def dump(name, src_ap, tok_list):
            if not DBG or name not in dbg:
                return
            for t in tok_list:
                S.wait("sync", t)
            S.inc("sync", "dO",
                  lambda e, d=dbg[name], s=src_ap: e.dma_start(out=d, in_=s), 16)
            for eng in ENGS:
                S.wait(eng, ("dO", S.cnt["dO"]))

        tok_c = []
        tok_c.append(S.inc("sync", "dC", lambda e: e.dma_start(out=mk0[:], in_=mk0_d), 16))
        tok_c.append(S.inc("sync", "dC", lambda e: e.dma_start(out=mki[:], in_=mki_d), 16))
        tok_c.append(S.inc("sync", "dC", lambda e: e.dma_start(out=cst[:], in_=cst_d), 16))
        A2f = A2[:].bitcast(F32)
        tok_lw = S.inc("sync", "dC", lambda e: e.dma_start(
            out=A2f[:, 4096:8192], in_=lw_d), 16)
        tok_c.append(tok_lw)
        tok_bs = S.inc("sync", "dC", lambda e: e.dma_start(
            out=A2f[:, 0:4096], in_=Bs_d), 16)
        tok_c.append(tok_bs)
        tok_const = list(tok_c)
        bsref = {}
        vRh = vR[:].rearrange("p s (h e) -> p s h e", h=8)
        tok_vones = S.inc("vector", "sVE", lambda e: e.memset(
            vRh[:, :, :, 128:129], 1.0))

        def make_bs16():
            S.wait("vector", tok_bs)
            S.wait("vector", tok_lw)
            t = ve_inc(lambda e: e.tensor_tensor(
                A2f[:, 0:4096], A2f[:, 0:4096], A2f[:, 4096:8192], op=ALU.mult))
            S.wait("scalar", t)
            tk = se_inc(lambda e: e.mul(Bs16[:], A2f[:, 0:4096], 2.0))
            bsref["tok"] = tk
            if DBG:
                dump("d_bs2", Bs16[:], [tk])
            return tk

        state = {
            "xsb_free": {0: None, 1: None},
            "lnt_free": {0: None, 1: None},
            "A1_free": None,
            "A2_free": None,
            "attn_done": None,
            "qT_ready": None,
            "kv_ready": None,
            "pTs_free": {0: None, 1: None, 2: None},
            "invS_free": {0: None, 1: None},
            "osb_free": None,
            "xstr_free": {0: None, 1: None, 2: None, 3: None},
            "strip_free": {0: None, 1: None, 2: None},
            "oS_free": {0: None, 1: None},
            "hT_ready": None,
        }
        pdt_state = {"par": 0}

        K_SX, K_SQ, K_MEAN, K_VAR, K_T1, K_RSTD, K_NEGMR, K_X2 = range(8)

        def stc(b, k):
            return st[:, 8 * b + k: 8 * b + k + 1]

        def x_preload(row0, b):
            S.wait("sync", state["xsb_free"][b])
            tok = S.inc("sync", "dX", lambda e, r=row0, b=b: e.dma_start(
                out=xsb[:, b, :], in_=xt[r:r + 128, :]), 16)
            state["xsb_free"][b] = ("LOADED", tok)
            return tok

        def ln_stats(row0, b, add_delta_tb=None, delta_ready=None, tokx=None):
            if tokx is None:
                S.wait("sync", state["xsb_free"][b])
                tokx = S.inc("sync", "dX", lambda e, r=row0, b=b: e.dma_start(
                    out=xsb[:, b, :], in_=xt[r:r + 128, :]), 16)
            H = D // 2
            if add_delta_tb is not None:
                S.wait("vector", tokx)
                S.wait("vector", delta_ready)
                ta1 = ve_inc(lambda e, b=b, tb=add_delta_tb: e.tensor_tensor(
                    xsb[:, b, 0:H], xsb[:, b, 0:H], dv[:, tb, 0:H], op=ALU.add))
                ta2 = ve_inc(lambda e, b=b, tb=add_delta_tb: e.tensor_tensor(
                    xsb[:, b, H:D], xsb[:, b, H:D], dv[:, tb, H:D], op=ALU.add))
            else:
                ta1 = ta2 = tokx
            S.wait("scalar", ta1)
            tsq1 = se_inc(lambda e, b=b: e.activation(
                sqs[:], xsb[:, b, 0:H], AF.Square,
                accum_out=stc(b, K_SQ)))
            S.wait("scalar", ta2)
            tsq2 = se_inc(lambda e, b=b: e.activation(
                sqs[:], xsb[:, b, H:D], AF.Square,
                accum_out=stc(b, K_VAR)))
            S.wait("vector", tsq1)
            tr1 = ve_inc(lambda e, b=b: e.reduce_sum(
                stc(b, K_SX), xsb[:, b, 0:H], axis=AX.X))
            S.wait("vector", tsq2)
            tr2 = ve_inc(lambda e, b=b: e.reduce_sum(
                stc(b, K_X2), xsb[:, b, H:D], axis=AX.X))
            S.wait("vector", tr1)
            S.wait("vector", tr2)
            t0 = ve_inc(lambda e, b=b: e.tensor_tensor(
                stc(b, K_SX), stc(b, K_SX), stc(b, K_X2), op=ALU.add))
            S.wait("vector", t0)
            t0b = ve_inc(lambda e, b=b: e.tensor_tensor(
                stc(b, K_SQ), stc(b, K_SQ), stc(b, K_VAR), op=ALU.add))
            S.wait("vector", t0b)
            t1 = ve_inc(lambda e, b=b: e.tensor_scalar_mul(
                stc(b, K_MEAN), stc(b, K_SX), 1.0 / D))
            t2 = ve_inc(lambda e, b=b: e.tensor_scalar_mul(
                stc(b, K_VAR), stc(b, K_SQ), 1.0 / D))
            S.wait("vector", t1)
            S.wait("vector", t2)
            t3 = ve_inc(lambda e, b=b: e.tensor_tensor(
                stc(b, K_T1), stc(b, K_MEAN), stc(b, K_MEAN), op=ALU.mult))
            S.wait("vector", t3)
            t4 = ve_inc(lambda e, b=b: e.tensor_scalar(
                stc(b, K_T1), stc(b, K_T1), -1.0, EPS, op0=ALU.mult, op1=ALU.add))
            S.wait("vector", t4)
            tok_var = ve_inc(lambda e, b=b: e.tensor_tensor(
                stc(b, K_VAR), stc(b, K_VAR), stc(b, K_T1), op=ALU.add))
            S.wait("scalar", tok_var)
            tok_ln = se_inc(lambda e, b=b: e.activation(
                stc(b, K_T1), stc(b, K_VAR), AF.Ln))
            S.wait("scalar", tok_ln)
            tok_rstd = se_inc(lambda e, b=b: e.activation(
                stc(b, K_RSTD), stc(b, K_T1), AF.Exp, scale=-0.5))
            S.wait("vector", tok_rstd)
            tok_negmr = ve_inc(lambda e, b=b: e.tensor_scalar(
                stc(b, K_NEGMR), stc(b, K_MEAN), stc(b, K_RSTD), -1.0,
                op0=ALU.mult, op1=ALU.mult))
            return (tok_rstd, tok_negmr)

        def ln_norm(b, tok_stats):
            tok_rstd, tok_negmr = tok_stats
            H = D // 2
            S.wait("scalar", tok_negmr)
            S.wait("scalar", tok_rstd)
            S.wait("scalar", state["lnt_free"][b])
            tn1 = se_inc(lambda e, b=b: e.activation(
                lnt[:, b, 0:H], xsb[:, b, 0:H], AF.Identity,
                scale=stc(b, K_RSTD), bias=stc(b, K_NEGMR)))
            tn2 = se_inc(lambda e, b=b: e.activation(
                lnt[:, b, H:D], xsb[:, b, H:D], AF.Identity,
                scale=stc(b, K_RSTD), bias=stc(b, K_NEGMR)))
            state["xsb_free"][b] = tn2
            return (tn1, tn2)

        def ln_front(row0, b, add_delta_tb=None, delta_ready=None, tokx=None):
            return ln_norm(b, ln_stats(row0, b, add_delta_tb=add_delta_tb,
                                       delta_ready=delta_ready, tokx=tokx))

        def ln_back(b, toff, tok_norm, a1_ready=None):
            tn1, tn2 = tok_norm
            last_ev = None
            tok_tr = None
            for g8 in range(4):
                par = pdt_state["par"]
                pdt_state["par"] = 1 - par
                S.wait("tensor", tn1 if g8 < 2 else tn2)
                if a1_ready is not None and g8 == 0:
                    S.wait("tensor", a1_ready)
                wait_banks("tensor", [PDT_BANK[par]])
                for kk in range(8):
                    kb = g8 * 8 + kk
                    fn = (lambda e, par=par, kk=kk, b=b, kb=kb: e.transpose(
                        pD_t[par][:, kk * 128:(kk + 1) * 128],
                        lnt[:, b, kb * 128:(kb + 1) * 128], ident))
                    if kk == 7:
                        tok_tr = pe_inc(fn)
                    else:
                        pe(fn)
                S.wait("vector", tok_tr)
                if a1_ready is not None and g8 == 0:
                    S.wait("vector", a1_ready)
                tok_ev = ve_inc(lambda e, par=par, g8=g8, toff=toff: e.tensor_copy(
                    A1[:, g8 * 8:(g8 + 1) * 8, toff:toff + 128],
                    pD_t[par][:].rearrange("p (k c) -> p k c", k=8)))
                bank_free[PDT_BANK[par]] = tok_ev
                last_ev = tok_ev
            state["lnt_free"][b] = tok_tr
            return last_ev

        def ln_tile(row0, b, toff, a1_ready=None, add_delta_tb=None,
                    delta_ready=None, tokx=None):
            tok_norm = ln_front(row0, b, add_delta_tb=add_delta_tb,
                                delta_ready=delta_ready, tokx=tokx)
            return ln_back(b, toff, tok_norm, a1_ready=a1_ready)

        def kv_phase(s0, ln_ready, wmode="fp32"):
            pp = ["pA0", "pB0"]
            last_ev = None
            for kd in range(8):
                idx, half, tok_arr = sched_slab(wk_d[kd], 4096, uid=kd,
                                                mode=wmode)
                key = pp[kd % 2]
                S.wait("tensor", tok_arr)
                if kd == 0:
                    S.wait("tensor", ln_ready)
                wait_banks("tensor", [key])
                wv3 = wview_q(half)
                tok_mm = None
                for K in range(32):
                    fn = (lambda e, key=key, wv3=wv3, K=K: e.matmul(
                        P512[key], wv3[:, K, :], A1[:, K, 0:512],
                        start=(K == 0), stop=(K == 31)))
                    if K == 31:
                        tok_mm = pe_inc(fn)
                    else:
                        pe(fn)
                slab_done(idx, tok_mm)
                S.wait("vector", tok_mm)
                S.wait("vector", state["attn_done"])
                tok_ev = ve_inc(lambda e, kd=kd, s0=s0, key=key: e.tensor_copy(
                    kT[:, kd, s0 * 128: s0 * 128 + 512], P512[key]))
                bank_free[key] = tok_ev
                last_ev = tok_ev
            vkeys = ["pC0", "pC1", "pD0", "pD1"]
            for vd in range(2):
                tok_mm = None
                for h2 in range(4):
                    idx, half, tok_arr = sched_slab(
                        wv_d[vd][:, 8 * h2:8 * h2 + 8, :], 4096,
                        uid=8 + vd * 4 + h2, mode=wmode)
                    S.wait("tensor", tok_arr)
                    if h2 == 0:
                        wait_banks("tensor", vkeys)
                    wv3 = wview_b8(half)
                    for kk in range(8):
                        K = 8 * h2 + kk
                        for tb in range(4):
                            fn = (lambda e, tb=tb, wv3=wv3, kk=kk, K=K: e.matmul(
                                P512[vkeys[tb]], A1[:, K, tb * 128:(tb + 1) * 128],
                                wv3[:, kk, :],
                                start=(K == 0), stop=(K == 31)))
                            if kk == 7 and tb == 3:
                                tok_mm = pe_inc(fn)
                            else:
                                pe(fn)
                    slab_done(idx, tok_mm)
                S.wait("vector", tok_mm)
                S.wait("vector", state["attn_done"])
                for tb in range(4):
                    slot = (s0 + tb) % 8
                    tok_ev = ve_inc(lambda e, tb=tb, slot=slot, vd=vd: e.tensor_copy(
                        vRh[:, slot, vd * 4:(vd + 1) * 4, 0:128],
                        P512[vkeys[tb]].rearrange("p (h e) -> p h e", h=4)))
                    bank_free[vkeys[tb]] = tok_ev
                    last_ev = tok_ev
            return last_ev

        def q_phase(ln_ready, wmode="fp32"):
            pp = ["pA0", "pB0"]
            last_ev = None
            for qb in range(32):
                idx, half, tok_arr = sched_slab(wq_d[qb], 4096, uid=16 + qb,
                                                mode=wmode)
                key = pp[qb % 2]
                S.wait("tensor", tok_arr)
                if qb == 0:
                    S.wait("tensor", ln_ready)
                wait_banks("tensor", [key])
                wv3 = wview_q(half)
                tok_mm = None
                for K in range(32):
                    fn = (lambda e, key=key, wv3=wv3, K=K: e.matmul(
                        P512[key], wv3[:, K, :], A1[:, K, 0:512],
                        start=(K == 0), stop=(K == 31)))
                    if K == 31:
                        tok_mm = pe_inc(fn)
                    else:
                        pe(fn)
                slab_done(idx, tok_mm)
                S.wait("vector", tok_mm)
                if qb == 0:
                    S.wait("vector", state["A2_free"])
                tok_ev = ve_inc(lambda e, qb=qb, key=key: e.tensor_copy(
                    qTv[:, qb, :], P512[key]))
                bank_free[key] = tok_ev
                last_ev = tok_ev
            return last_ev

        def attn_phase(c):
            tok_osb_tr = None
            last_tr_ev = None
            box = {"tok_pv_final": None, "tok_last_oev": None,
                   "first_evac_done": False}
            toks_exp = {}

            def emit_qk(a, i):
                par = i % 3
                sk0, sk1 = STRIP_BANKS[par]
                S.wait("tensor", state["strip_free"][par])
                wait_banks("tensor", [sk0, sk1])
                if a == 0 and i == 0:
                    S.wait("tensor", state["qT_ready"])
                    S.wait("tensor", state["kv_ready"])
                kvh = i // 4
                hq = i
                tok_qk = None
                for j in range(5):
                    slot = (4 * c + a + j) % 8
                    fn = (lambda e, par=par, j=j, slot=slot, kvh=kvh, hq=hq,
                          a=a: e.matmul(
                              strip[par][:, j * 128:(j + 1) * 128],
                              kT[:, kvh, slot * 128:(slot + 1) * 128],
                              qTv[:, hq, a * 128:(a + 1) * 128],
                              start=True, stop=True))
                    if j == 4:
                        tok_qk = pe_inc(fn)
                    else:
                        pe(fn)
                S.wait("vector", tok_qk)
                if c == 0:
                    tok_mask = None
                    for j in range(5):
                        tok_mask = ve_inc(lambda e, par=par, j=j, a=a:
                                          e.tensor_tensor(
                            strip[par][:, j * 128:(j + 1) * 128],
                            strip[par][:, j * 128:(j + 1) * 128],
                            mk0[:, a, j, :], op=ALU.add))
                else:
                    ve_inc(lambda e, par=par: e.tensor_tensor(
                        strip[par][:, 0:128], strip[par][:, 0:128],
                        mki[:, 0, :], op=ALU.add))
                    tok_mask = ve_inc(lambda e, par=par: e.tensor_tensor(
                        strip[par][:, 512:640], strip[par][:, 512:640],
                        mki[:, 1, :], op=ALU.add))
                S.wait("scalar", tok_mask)
                S.wait("scalar", state["pTs_free"][par])
                tok_exp = se_inc(lambda e, par=par: e.activation(
                    pTs[:, par, :], strip[par][:, 0:640], AF.Exp,
                    scale=ASCALE))
                state["strip_free"][par] = tok_exp
                bank_free[sk0] = tok_exp
                bank_free[sk1] = tok_exp
                toks_exp[i] = tok_exp

            def emit_pv(a, i):
                par = i % 3
                opar = i % 2
                ob = OS_BANK[opar]
                S.wait("tensor", toks_exp[i])
                S.wait("tensor", state["oS_free"][opar])
                wait_banks("tensor", [ob])
                kvh = i // 4
                hq = i
                tok_pv = None
                for j in range(5):
                    slot = (4 * c + a + j) % 8
                    fn = (lambda e, par=par, opar=opar, j=j, slot=slot, kvh=kvh:
                          e.matmul(
                              o_ps[opar], pTs[:, par, j * 128:(j + 1) * 128],
                              vR[:, slot, kvh * 129:kvh * 129 + 129],
                              start=(j == 0), stop=(j == 4)))
                    if j == 4:
                        tok_pv = pe_inc(fn)
                    else:
                        pe(fn)
                state["pTs_free"][par] = tok_pv
                S.wait("vector", tok_pv)
                S.wait("vector", state["invS_free"][opar])
                tok_inv = ve_inc(lambda e, opar=opar: e.reciprocal(
                    invS[:, opar:opar + 1], S_ps[opar]))
                S.wait("scalar", tok_inv)
                if a == 0 and i == 0:
                    S.wait("scalar", state["osb_free"])
                if not box["first_evac_done"] and tok_osb_tr is not None:
                    S.wait("scalar", tok_osb_tr)
                box["first_evac_done"] = True
                tok_oev = se_inc(lambda e, opar=opar, hq=hq: e.activation(
                    osb[:, hq * 128:(hq + 1) * 128],
                    o_ps[opar][:, 0:128],
                    AF.Identity, scale=invS[:, opar:opar + 1]))
                state["invS_free"][opar] = tok_oev
                state["oS_free"][opar] = tok_oev
                bank_free[ob] = tok_oev
                box["tok_last_oev"] = tok_oev
                box["tok_pv_final"] = tok_pv

            tr_areas = [(pA16, 0, "pA0"), (pA16, 1024, "pA1"),
                        (pB16, 0, "pB0"), (pB16, 1024, "pB1")]
            for a in range(4):
                box["first_evac_done"] = False
                for i in range(32):
                    emit_qk(a, i)
                    if i >= 2:
                        emit_pv(a, i - 2)
                emit_pv(a, 30)
                emit_pv(a, 31)
                tok_last_oev = box["tok_last_oev"]
                tok_tr = None
                for g8 in range(4):
                    buf16, off, bkey = tr_areas[g8]
                    S.wait("tensor", tok_last_oev)
                    if a == 0 and g8 == 0:
                        S.wait("tensor", state["A1_free"])
                    wait_banks("tensor", [bkey])
                    for kk in range(8):
                        kb = g8 * 8 + kk
                        fn = (lambda e, buf16=buf16, off=off, kk=kk, kb=kb:
                              e.transpose(
                                  buf16[:, off + kk * 128: off + (kk + 1) * 128],
                                  osb[:, kb * 128:(kb + 1) * 128], ident))
                        if kk == 7:
                            tok_tr = pe_inc(fn)
                        else:
                            pe(fn)
                    S.wait("vector", tok_tr)
                    if a == 0 and g8 == 0:
                        S.wait("vector", state["A1_free"])
                    tok_ev = ve_inc(lambda e, buf16=buf16, off=off, g8=g8, a=a:
                                    e.tensor_copy(
                        A1[:, g8 * 8:(g8 + 1) * 8, a * 128:(a + 1) * 128],
                        buf16[:, off:off + 1024].rearrange(
                            "p (k c) -> p k c", k=8)))
                    bank_free[bkey] = tok_ev
                    last_tr_ev = tok_ev
                tok_osb_tr = tok_tr
            state["osb_free"] = tok_osb_tr
            state["attn_done"] = box["tok_pv_final"]
            return last_tr_ev, box["tok_pv_final"]

        def wo_phase(oT_ready, qT_done, wmode="fp32"):
            groups = [["pA0", "pA1", "pB0", "pB1"], ["pC0", "pC1", "pD0", "pD1"]]
            last_ev = None
            for og in range(8):
                keys = groups[og % 2]
                tok_mm = None
                for h2 in range(4):
                    idx, half, tok_arr = sched_slab(
                        wo_d[og][:, 8 * h2:8 * h2 + 8, :], 4096,
                        uid=48 + og * 4 + h2, mode=wmode)
                    S.wait("tensor", tok_arr)
                    if og == 0 and h2 == 0:
                        S.wait("tensor", oT_ready)
                    if h2 == 0:
                        wait_banks("tensor", keys)
                    wv3 = wview_b8(half)
                    for kk in range(8):
                        K = 8 * h2 + kk
                        for tb in range(4):
                            fn = (lambda e, tb=tb, kk=kk, K=K, wv3=wv3,
                                  keys=keys: e.matmul(
                                      P512[keys[tb]],
                                      A1[:, K, tb * 128:(tb + 1) * 128],
                                      wv3[:, kk, :],
                                      start=(K == 0), stop=(K == 31)))
                            if kk == 7 and tb == 3:
                                tok_mm = pe_inc(fn)
                            else:
                                pe(fn)
                    slab_done(idx, tok_mm)
                S.wait("vector", tok_mm)
                if og == 0:
                    S.wait("vector", qT_done)
                for tb in range(4):
                    tok_ev = ve_inc(lambda e, tb=tb, og=og, keys=keys: e.tensor_copy(
                        dv[:, tb, og * 512:(og + 1) * 512], P512[keys[tb]]))
                    bank_free[keys[tb]] = tok_ev
                    last_ev = tok_ev
            return last_ev

        def ffn_preload(wmode="fp32"):
            S.wait("gpsimd", state["osb_free"])
            tok_bias = S.inc("gpsimd", "dW", lambda e: e.dma_start(
                out=osb[0:1, :], in_=bias_d), 16)
            aslab = sched_slab(At_d, 4096, uid=80, mode=wmode)
            slabs = []
            for og in range(8):
                for h2 in range(4):
                    slabs.append(sched_slab(
                        Wt_d[og][:, 8 * h2:8 * h2 + 8, :], 4096,
                        uid=81 + og * 4 + h2, mode=wmode))
            return tok_bias, aslab, slabs

        def ffn_phase(c, ln2b, delta_ready, preload):
            tok_bias, aslab, slabs = preload
            aidx, ahalf, tok_aarr = aslab
            av3 = wview_q(ahalf)
            tok_hev = None

            groups = [["pA0", "pA1", "pB0", "pB1"], ["pC0", "pC1", "pD0", "pD1"]]
            stripe_i = 0
            tok_lora_last = None
            last_out = None
            for og in range(8):
                keys = groups[og % 2]
                wait_banks("tensor", keys)
                if og == 0:
                    S.wait("tensor", delta_ready)
                for tb in range(4):
                    pe(lambda e, tb=tb, og=og, keys=keys: e.matmul(
                        P512[keys[tb]], ident,
                        dv[:, tb, og * 512:(og + 1) * 512],
                        start=True, stop=False))
                if not NOBIAS:
                    if og == 0:
                        S.wait("tensor", tok_bias)
                    for tb in range(4):
                        pe(lambda e, tb=tb, og=og, keys=keys: e.matmul(
                            P512[keys[tb]], ones_row,
                            osb[0:1, og * 512:(og + 1) * 512],
                            start=False, stop=False))
                for h2 in range(4):
                    idx, half, tok_arr = slabs[og * 4 + h2]
                    S.wait("tensor", tok_arr)
                    wv3 = wview_b8(half)
                    tok_mm = None
                    for kk in range(8):
                        K = 8 * h2 + kk
                        for tb in range(4):
                            if og == 0 and h2 == 0 and kk == 0:
                                S.wait("tensor", ln2b[tb])
                            fn = (lambda e, tb=tb, kk=kk, K=K, wv3=wv3,
                                  keys=keys: e.matmul(
                                      P512[keys[tb]],
                                      A1[:, K, tb * 128:(tb + 1) * 128],
                                      wv3[:, kk, :],
                                      start=False, stop=False))
                            if kk == 7 and tb == 3:
                                tok_mm = pe_inc(fn)
                            else:
                                pe(fn)
                    slab_done(idx, tok_mm)
                    if og == 0 and h2 == 0:
                        S.wait("tensor", tok_aarr)
                        wait_banks("tensor", ["pC0"])
                        tok_h = None
                        for K in range(32):
                            fn = (lambda e, K=K, av3=av3: e.matmul(
                                P512["pC0"], av3[:, K, :], A1[:, K, 0:512],
                                start=(K == 0), stop=(K == 31)))
                            if K == 31:
                                tok_h = pe_inc(fn)
                            else:
                                pe(fn)
                        slab_done(aidx, tok_h)
                        S.wait("vector", tok_h)
                        S.wait("vector", state["hT_ready"])
                        tok_hev = ve_inc(lambda e: e.tensor_copy(
                            hT[:], P512["pC0"]))
                        bank_free["pC0"] = tok_hev
                        if DBG and c == DBGC:
                            dump("d_hT", hT[:], [tok_hev])
                tok_grp = None
                if og == 0:
                    S.wait("tensor", tok_hev)
                for tb in range(4):
                    fn = (lambda e, tb=tb, og=og, keys=keys: e.matmul(
                        P512[keys[tb]], hT[:, tb * 128:(tb + 1) * 128],
                        Bs16[:, og * 512:(og + 1) * 512],
                        start=False, stop=True))
                    if tb == 3:
                        tok_grp = pe_inc(fn)
                    else:
                        pe(fn)
                tok_lora_last = tok_grp
                if DBG and c == DBGC and og == 0:
                    lnt32 = lnt[:, 0, :].bitcast(F32)
                    tks = []
                    for tb in range(4):
                        S.wait("vector", tok_grp)
                        tks.append(ve_inc(lambda e, tb=tb, keys=keys, lnt32=lnt32:
                                          e.tensor_copy(lnt32[:, tb * 512:(tb + 1) * 512],
                                                        P512[keys[tb]])))
                    dump("d_p7", lnt32, tks)
                    dump("d_bs", Bs16[:], [])
                    dump("d_brow", osb[:], [])
                for tb in range(4):
                    si = stripe_i % 4
                    stripe_i += 1
                    row0 = HALO + c * CH + tb * 128
                    S.wait("sync", state["xstr_free"][si])
                    tokx = S.inc("sync", "dX",
                                 lambda e, si=si, row0=row0, og=og: e.dma_start(
                                     out=xstr[:, si, :],
                                     in_=xt[row0:row0 + 128,
                                            og * 512:(og + 1) * 512]), 16)
                    S.wait("vector", tokx)
                    S.wait("vector", tok_grp)
                    tok_st = ve_inc(lambda e, si=si, tb=tb, keys=keys: e.tensor_tensor(
                        xstr[:, si, :], xstr[:, si, :], P512[keys[tb]],
                        op=ALU.add))
                    bank_free[keys[tb]] = tok_st
                    S.wait("sync", tok_st)
                    row0o = c * CH + tb * 128
                    tok_out = S.inc("sync", "dO",
                                    lambda e, si=si, row0o=row0o, og=og:
                                    e.dma_start(
                                        out=out_d[row0o:row0o + 128,
                                                  og * 512:(og + 1) * 512],
                                        in_=xstr[:, si, :]), 16)
                    state["xstr_free"][si] = tok_out
                    last_out = tok_out
            state["hT_ready"] = tok_lora_last
            state["osb_free"] = tok_lora_last
            return last_out, tok_lora_last

        for tok in tok_const:
            S.wait("tensor", tok)
            S.wait("vector", tok)
            S.wait("scalar", tok)

        last_ev = None
        for hb in range(4):
            last_ev = ln_tile(hb * 128, hb % 2, hb * 128)
        if DBG:
            dump("d_hln", A1[:].rearrange("p a b -> p (a b)"), [last_ev])
        tok_kv = kv_phase(0, last_ev)
        state["kv_ready"] = tok_kv
        state["A1_free"] = tok_kv
        tok_bs16 = make_bs16()
        state["A2_free"] = tok_bs16
        for eng in ("tensor",):
            S.wait(eng, tok_bs16)
        if DBG:
            dump("d_kT0", kT[:].rearrange("p a b -> p (a b)"), [tok_kv])

        for c in range(NCHUNK):
            if DBG and c == DBGC + 1:
                for eng in ENGS:
                    S.wait(eng, ("dO", S.cnt["dO"]))
            last_ev = None
            if c == 0:
                for tb in range(4):
                    a1r = state["A1_free"] if tb == 0 else None
                    last_ev = ln_tile(HALO + c * CH + tb * 128, tb % 2,
                                      tb * 128, a1_ready=a1r)
            else:
                ln_back(0, 0, frontsP1[0], a1_ready=state["A1_free"])
                fr2 = ln_norm(0, statsP1[0])
                ln_back(1, 128, frontsP1[1])
                fr3 = ln_norm(1, statsP1[1])
                ln_back(0, 256, fr2)
                last_ev = ln_back(1, 384, fr3)
            ln_ready = last_ev
            if DBG and c == DBGC:
                dump("d_ln1T", A1[:].rearrange("p a b -> p (a b)"), [ln_ready])
            s0 = (4 + 4 * c) % 8
            tok_kv = kv_phase(s0, ln_ready,
                              wmode=("wb" if c == 0 else "bf16"))
            state["kv_ready"] = tok_kv
            tok_q = q_phase(ln_ready, wmode=("wb" if c == 0 else "bf16"))
            state["qT_ready"] = tok_q
            if DBG and c == DBGC:
                dump("d_kT", kT[:].rearrange("p a b -> p (a b)"), [tok_kv])
                dump("d_v", vR[:].rearrange("p a b -> p (a b)"), [tok_kv])
                dump("d_qT", A2[:, 0:16384], [tok_q])
            state["A1_free"] = tok_q
            oT_ready, tok_attn = attn_phase(c)
            if DBG and c == DBGC:
                dump("d_osb", osb[:], [state["oS_free"][0], state["oS_free"][1]])
                dump("d_oT", A1[:].rearrange("p a b -> p (a b)"), [oT_ready])
            tokx6 = [x_preload(HALO + c * CH + tb * 128, tb % 2)
                     for tb in range(2)] + [None, None]
            tok_delta = wo_phase(oT_ready, tok_attn,
                                 wmode=("wb" if c == 0 else "bf16"))
            if DBG and c == DBGC:
                dump("d_delta", A2[:, 0:16384], [tok_delta])
            state["A1_free"] = tok_delta
            preload = ffn_preload(wmode=("wb" if c == 0 else "bf16"))
            ln2b = []
            for tb in range(4):
                a1r = state["A1_free"] if tb == 0 else None
                ln2b.append(ln_tile(HALO + c * CH + tb * 128, tb % 2, tb * 128,
                                    a1_ready=a1r, add_delta_tb=tb,
                                    delta_ready=tok_delta, tokx=tokx6[tb]))
            ln2_ready = ln2b[-1]
            if DBG and c == DBGC:
                dump("d_ln2T", A1[:].rearrange("p a b -> p (a b)"), [ln2_ready])
            if c + 1 < NCHUNK:
                fr01 = []
                for tb in range(2):
                    tkx = x_preload(HALO + (c + 1) * CH + tb * 128, tb % 2)
                    fr01.append(ln_front(HALO + (c + 1) * CH + tb * 128,
                                         tb % 2, tokx=tkx))
                st23 = []
                for tb in (2, 3):
                    tkx = x_preload(HALO + (c + 1) * CH + tb * 128, tb % 2)
                    st23.append(ln_stats(HALO + (c + 1) * CH + tb * 128,
                                         tb % 2, tokx=tkx))
            else:
                fr01 = None
                st23 = None
            tok_out, tok_lora = ffn_phase(c, ln2b, tok_delta, preload)
            state["A1_free"] = tok_lora
            state["A2_free"] = tok_out
            frontsP1 = fr01
            statsP1 = st23

        S.wait("sync", ("dO", S.cnt["dO"]))

        with nc.Block() as block:
            @block.sync
            def _(e):
                for f in S.q["sync"]:
                    f(e)

            @block.gpsimd
            def _(e):
                for f in S.q["gpsimd"]:
                    f(e)

            @block.vector
            def _(e):
                for f in S.q["vector"]:
                    f(e)

            @block.scalar
            def _(e):
                for f in S.q["scalar"]:
                    f(e)

            @block.tensor
            def _(e):
                for f in S.q["tensor"]:
                    f(e)

    return nc



def _prep_shared(wq, wk, wv, wo, A, Bmat, W, bias, lw):
    d = {}
    d["wq"] = np.ascontiguousarray(
        wq.reshape(32, 128, 32, 128).transpose(0, 3, 2, 1))
    d["wk"] = np.ascontiguousarray(
        wk.reshape(8, 128, 32, 128).transpose(0, 3, 2, 1))
    d["wv"] = np.ascontiguousarray(
        wv.reshape(2, 512, 32, 128).transpose(0, 3, 2, 1))
    d["wo"] = np.ascontiguousarray(
        wo.reshape(8, 512, 32, 128).transpose(0, 3, 2, 1))
    d["Wt"] = np.ascontiguousarray(
        W.reshape(8, 512, 32, 128).transpose(0, 3, 2, 1))
    acat = A.reshape(128, 4096)
    d["At"] = np.ascontiguousarray(acat.T.reshape(32, 128, 128).transpose(1, 0, 2))
    d["Bs"] = np.ascontiguousarray(Bmat.transpose(0, 2, 1).reshape(128, 4096))
    d["lw"] = np.ascontiguousarray(np.broadcast_to(
        np.repeat(lw, 16).astype(np.float32).reshape(128, 1), (128, 4096)))
    d["bias"] = np.ascontiguousarray(bias.reshape(1, 4096))
    cstv = np.zeros((128, 256), dtype=BF)
    cstv[:, 0:128] = np.eye(128, dtype=BF)
    cstv[:, 128:256] = np.ones((128, 128), dtype=BF)
    d["cst"] = cstv
    return d


def _masks(h):
    cc = np.arange(128)[:, None]
    aa = np.arange(128)[None, :]
    edge = np.where(cc >= aa + 1, 0.0, NEG).astype(np.float32)
    diag = np.where(cc <= aa, 0.0, NEG).astype(np.float32)
    full = np.full((128, 128), NEG, dtype=np.float32)
    none = np.zeros((128, 128), dtype=np.float32)
    mki = np.stack([edge, diag], axis=1).astype(BF)
    mk0 = np.zeros((128, 4, 5, 128), dtype=np.float32)
    for a in range(4):
        for j in range(5):
            if h == 1:
                m = edge if j == 0 else (diag if j == 4 else none)
            else:
                if j == 4:
                    m = diag
                elif j <= 3 - a:
                    m = full
                else:
                    m = none
            mk0[:, a, j, :] = m
    return mk0.astype(BF), mki


def kernel(L0_attn_x, L0_attn_wq, L0_attn_wk, L0_attn_wv, L0_attn_wo,
           L0_ffn_lora_A, L0_ffn_lora_B, L0_ffn_W, L0_ffn_bias,
           L0_ffn_lora_weights):
    x = np.asarray(L0_attn_x, dtype=np.float32)
    shared = _prep_shared(np.asarray(L0_attn_wq, np.float32),
                          np.asarray(L0_attn_wk, np.float32),
                          np.asarray(L0_attn_wv, np.float32),
                          np.asarray(L0_attn_wo, np.float32),
                          np.asarray(L0_ffn_lora_A, np.float32),
                          np.asarray(L0_ffn_lora_B, np.float32),
                          np.asarray(L0_ffn_W, np.float32),
                          np.asarray(L0_ffn_bias, np.float32),
                          np.asarray(L0_ffn_lora_weights, np.float32))
    mk0_h0, mki = _masks(0)
    mk0_h1, _ = _masks(1)

    in_maps = []
    for core in range(8):
        b, h = core // 2, core % 2
        t0 = h * TCORE
        if h == 0:
            halo = np.zeros((HALO, D), dtype=np.float32)
        else:
            halo = x[b, t0 - HALO:t0]
        xtok = np.ascontiguousarray(
            np.concatenate([halo, x[b, t0:t0 + TCORE]], axis=0))
        m = dict(shared)
        m["xt"] = xtok
        m["mk0"] = mk0_h0 if h == 0 else mk0_h1
        m["mki"] = mki
        in_maps.append(m)

    nc = build_nc()
    res = run_bass_kernel_spmd(nc, in_maps, core_ids=list(range(8)))

    outf = np.empty((4, 4096, 4096), dtype=np.float32)
    for core in range(8):
        b, h = core // 2, core % 2
        outf[b, h * TCORE:(h + 1) * TCORE] = res.results[core]["out"]
    return outf


